# version 01
# baseline (speedup 1.0000x reference)
import contextlib
import numpy as np
import concourse.bass as bass
import concourse.mybir as mybir
from concourse.bass_utils import run_bass_kernel_spmd

F32 = mybir.dt.float32
BF16 = mybir.dt.bfloat16
F16 = mybir.dt.float16
I32 = mybir.dt.int32
AF = mybir.ActivationFunctionType
OP = mybir.AluOpType
AX = mybir.AxisListType.X

S = 4096
D = 2048
H = 16
HD = 128
FI = 5632
OGU = 2 * FI
NKB = D // 128
IKB = FI // 128
NT = S // 128
NG = 512
NGRP = S // NG
NPAIR = 11
NGA = 1024
LN_EPS = 1e-5
RMS_EPS = 1e-6
ISCALE = float(1.0 / np.sqrt(128.0))

ENGS = ("sync", "gpsimd", "vector", "scalar", "tensor")
DQ_NG = 88
DQ_ND = 16
DQ_SCALES_ONLY = False
FFN_NGRP = NGRP
FFN_PARTS = ("p1", "p2", "tails")
FFN_P2_MODE = "full"
DBG_FFN = False
A2_HEADS = H
A2_QCS = 8
DBG_HEAD = 0
DQ_N_SC = 104
SKIP_BIAS = False


class Sem:
    __slots__ = ("h", "n", "name")

    def __init__(self, h, name):
        self.h = h
        self.n = 0
        self.name = name


class B:
    def __init__(self, nc, ctx):
        self.nc = nc
        self.ctx = ctx
        self.tape = {e: [] for e in ENGS}

    def sem(self, name):
        return Sem(self.ctx.enter_context(self.nc.semaphore(name)), name)

    def emit(self, eng, f, waits=(), inc=None, inc_k=1):
        ws = tuple((s.h, v) for (s, v) in waits if v > 0)
        if inc is not None:
            inc.n += inc_k
        inc_h = inc.h if inc is not None else None

        def closure(e, f=f, ws=ws, inc_h=inc_h, inc_k=inc_k):
            for (h, v) in ws:
                e.wait_ge(h, v)
            ins = f(e)
            if inc_h is not None:
                ins.then_inc(inc_h, inc_k)
        self.tape[eng].append(closure)

    def wait(self, eng, s, v):
        if v > 0:
            self.tape[eng].append(lambda e, h=s.h, v=v: e.wait_ge(h, v))


class Arena:
    def __init__(self, nc, base, top):
        self.nc = nc
        self.base = base
        self.top = top
        self.off = base
        self.n = 0

    def alloc(self, shape, dtype, name=None):
        bpe = {F32: 4, BF16: 2, F16: 2, I32: 4}[dtype]
        nbytes = shape[1] * bpe
        off = (self.off + 31) & ~31
        assert off + nbytes <= self.top, (
            f"SBUF arena overflow: {name} needs {nbytes}B at {off}, top {self.top}")
        self.off = off + nbytes
        self.n += 1
        t = self.nc.alloc_sbuf_tensor_at(
            name or f"ar{self.n}", shape, dtype, offset=off)
        return t.ap()

    def mark(self):
        return self.off

    def reset(self, mark):
        self.off = mark


def build_nc(phases=("l1", "dq", "ffn1", "a1", "a2", "a3", "ffn2"), debug_out=None):
    nc = bass.Bass("TRN2", target_bir_lowering=False, detect_race_conditions=False)

    x_in = nc.dram_tensor("x", [S, D], F32, kind="ExternalInput").ap()
    wq = nc.dram_tensor("wq", [D, D], F32, kind="ExternalInput").ap()
    wk = nc.dram_tensor("wk", [D, D], F32, kind="ExternalInput").ap()
    wv = nc.dram_tensor("wv", [D, D], F32, kind="ExternalInput").ap()
    wo = nc.dram_tensor("wo", [D, D], F32, kind="ExternalInput").ap()
    bq = nc.dram_tensor("bq", [1, D], F32, kind="ExternalInput").ap()
    bk = nc.dram_tensor("bk", [1, D], F32, kind="ExternalInput").ap()
    bv = nc.dram_tensor("bv", [1, D], F32, kind="ExternalInput").ap()
    bo = nc.dram_tensor("bo", [1, D], F32, kind="ExternalInput").ap()
    gup = nc.dram_tensor("gup", [OGU, D // 8], I32, kind="ExternalInput").ap()
    gus = nc.dram_tensor("gus", [NKB, OGU], F16, kind="ExternalInput").ap()
    dnp = nc.dram_tensor("dnp", [D, FI // 8], I32, kind="ExternalInput").ap()
    dns = nc.dram_tensor("dns", [IKB, D], F16, kind="ExternalInput").ap()
    out = nc.dram_tensor("out", [S, D], F32, kind="ExternalOutput").ap()

    wbf = {w: nc.dram_tensor(f"w{w}b", [D, D], BF16, kind="Internal").ap()
           for w in ("q", "k", "v", "o")}
    ln1 = nc.dram_tensor("ln1", [S, D], BF16, kind="Internal").ap()
    ln2 = nc.dram_tensor("ln2", [S, D], BF16, kind="Internal").ap()
    ln3 = nc.dram_tensor("ln3", [S, D], BF16, kind="Internal").ap()
    x1 = nc.dram_tensor("x1", [S, D], F32, kind="Internal").ap()
    x2 = nc.dram_tensor("x2", [S, D], F32, kind="Internal").ap()
    wguT = nc.dram_tensor("wguT", [22, 128, 8192], BF16, kind="Internal").ap()
    wdnT = nc.dram_tensor("wdnT", [8, 128, IKB * 256], BF16, kind="Internal").ap()
    qt_d = nc.dram_tensor("qt", [H, 128, S], BF16, kind="Internal").ap()
    kt_d = nc.dram_tensor("kt", [H, 128, S], BF16, kind="Internal").ap()
    v_d = nc.dram_tensor("vv", [NT, 128, D], BF16, kind="Internal").ap()
    mgd = nc.dram_tensor("mgd", [H, 128, S], BF16, kind="Internal").ap()
    dbg_hT = nc.dram_tensor("dbg_hT", [128, IKB * NG], BF16, kind="Internal").ap()
    dbg_lnT = nc.dram_tensor("dbg_lnT", [128, NKB * NG], BF16, kind="Internal").ap()
    dbg_y = nc.dram_tensor("dbg_y", [128, 4 * D], F32, kind="Internal").ap()
    dbg_sq = nc.dram_tensor("dbg_sq", [128, 96], F32, kind="Internal").ap()

    dbg_map = {"x1": (x1, F32), "ln1": (ln1, BF16), "ln2": (ln2, BF16),
               "ln3": (ln3, BF16), "x2": (x2, F32), "wgut": (None, None), "scg": (None, None), "wt": (None, None), "ht": (None, None), "y": (None, None), "sqi": (None, None), "qt0": (None, None), "kt0": (None, None), "v0": (None, None), "mgd0": (None, None), "mgda": (None, None)}

    with contextlib.ExitStack() as ctx:
        b = B(nc, ctx)

        psF = nc.alloc_psum_tensor("psF", [128, 3072], F32).ap()
        psB = nc.alloc_psum_tensor("psB", [128, 2048], BF16).ap()

        sbase = 16896
        stop_ = nc.SBUF_PARTITION_SIZE_BYTES - 64
        static = Arena(nc, sbase, stop_)
        ident_b = static.alloc([128, 128], BF16, "ident_b")
        ident_f = static.alloc([128, 128], F32, "ident_f")
        iota_i = static.alloc([128, 128], I32, "iota_i")
        ln05 = static.alloc([128, 1], F32, "ln05")
        isc = static.alloc([128, 1], F32, "isc")
        bq_sb = static.alloc([128, 16], F32, "bq_sb")
        bk_sb = static.alloc([128, 16], F32, "bk_sb")
        ones1 = static.alloc([1, 128], F32, "ones1")
        onesf = static.alloc([128, 128], F32, "onesf")
        arena = Arena(nc, static.off, stop_)

        sq = b.sem("sq")
        sc = b.sem("sc")
        sg = b.sem("sg")
        pv = b.sem("pv")
        ps_ = b.sem("ps")
        pt = b.sem("pt")
        pg = b.sem("pg")

        b.emit("gpsimd", lambda e: e.iota(iota_i[:], pattern=[[1, 128]], base=0,
                                          channel_multiplier=-1), inc=pg)
        b.emit("vector", lambda e: e.tensor_single_scalar(
            out=ident_b[:], in_=iota_i[:], scalar=0, op=OP.is_equal),
            waits=[(pg, pg.n)])
        b.emit("vector", lambda e: e.tensor_single_scalar(
            out=ident_f[:], in_=iota_i[:], scalar=0, op=OP.is_equal))
        b.emit("vector", lambda e: e.memset(ln05[:], float(np.log(0.5))))
        b.emit("vector", lambda e: e.memset(isc[:], ISCALE))
        b.emit("vector", lambda e: e.memset(ones1[:], 1.0))
        b.emit("vector", lambda e: e.memset(onesf[:], 1.0), inc=pv)
        pv_const = pv.n
        for (bias, sb) in (() if SKIP_BIAS else ((bq, bq_sb), (bk, bk_sb))):
            for dt_i in range(16):
                b.emit("sync", lambda e, bias=bias, sb=sb, dt_i=dt_i: e.dma_start(
                    out=sb[:, dt_i:dt_i + 1],
                    in_=bias[0, dt_i * 128:(dt_i + 1) * 128, None]),
                    inc=sq, inc_k=16)
        for w, src in (("q", wq), ("k", wk), ("v", wv), ("o", wo)):
            b.emit("gpsimd", lambda e, w=w, src=src: e.dma_start(
                out=wbf[w][:], in_=src[:]), inc=sg, inc_k=16)
        sg_wcast = sg.n

        def ln_tile(xs_ap, lo_ap, st, ready_waits, out_waits):
            b.emit("scalar", lambda e: e.activation(
                junkS[:], xs_ap, AF.Square, accum_out=st[:, 1:2]),
                waits=ready_waits, inc=ps_)
            sqd = ps_.n
            b.emit("vector", lambda e: e.reduce_sum(
                out=st[:, 0:1], in_=xs_ap, axis=AX), waits=ready_waits, inc=pv)
            b.emit("vector", lambda e: e.tensor_scalar(
                out=st[:, 2:3], in0=st[:, 0:1], scalar1=1.0 / D, scalar2=None,
                op0=OP.mult), waits=[(ps_, sqd), (pv, pv.n)], inc=pv)
            b.emit("vector", lambda e: e.tensor_scalar(
                out=st[:, 3:4], in0=st[:, 1:2], scalar1=1.0 / D, scalar2=LN_EPS,
                op0=OP.mult, op1=OP.add), inc=pv)
            b.emit("vector", lambda e: e.tensor_tensor(
                out=st[:, 4:5], in0=st[:, 2:3], in1=st[:, 2:3], op=OP.mult),
                waits=[(pv, pv.n)], inc=pv)
            b.emit("vector", lambda e: e.tensor_tensor(
                out=st[:, 5:6], in0=st[:, 3:4], in1=st[:, 4:5], op=OP.subtract),
                waits=[(pv, pv.n)], inc=pv)
            b.emit("scalar", lambda e: e.activation(
                st[:, 6:7], st[:, 5:6], AF.Ln), waits=[(pv, pv.n)])
            b.emit("scalar", lambda e: e.activation(
                st[:, 7:8], st[:, 6:7], AF.Exp, scale=-0.5), inc=ps_)
            b.emit("vector", lambda e: e.scalar_tensor_tensor(
                out=st[:, 8:9], in0=st[:, 2:3], scalar=-1.0, in1=st[:, 7:8],
                op0=OP.mult, op1=OP.mult), waits=[(ps_, ps_.n)], inc=pv)
            b.emit("vector", lambda e: e.tensor_copy(st[:, 9:10], st[:, 7:8]),
                   waits=[(pv, pv.n)], inc=pv)
            b.emit("scalar", lambda e: e.activation(
                lo_ap, xs_ap, AF.Identity, scale=st[:, 9:10], bias=st[:, 8:9]),
                waits=[(pv, pv.n)] + out_waits, inc=ps_)
            return ps_.n

        junkS = static.alloc([128, D], F32, "junkS")
        arena = Arena(nc, static.off, stop_)

        dbg_scg_holder = [None, None]
        DBG_WT = [out] if debug_out == "wt" else None
        dq_scT_pv = 0

        def entry_waits():
            return [(pv, pv.n), (ps_, ps_.n), (pt, pt.n), (sg, sg.n)]

        def phase_barrier():
            snap = {"pv": pv.n, "ps": ps_.n, "pt": pt.n, "pg": pg.n,
                    "sg": sg.n, "sq": sq.n, "sc": sc.n}
            for eng, own in (("vector", "pv"), ("scalar", "ps"),
                             ("tensor", "pt"), ("gpsimd", "pg")):
                for s, key in ((pv, "pv"), (ps_, "ps"), (pt, "pt"),
                               (sg, "sg"), (sq, "sq"), (sc, "sc")):
                    if key != own and snap[key] > 0:
                        b.wait(eng, s, snap[key])
            return entry_waits()

        l1_done_sg = sg.n
        if "l1" in phases:
            m0 = arena.mark()
            xsl = [arena.alloc([128, D], F32, f"l1x{i}") for i in range(3)]
            lnsl = [arena.alloc([128, D], BF16, f"l1o{i}") for i in range(2)]
            stt = arena.alloc([128, 32], F32, "l1stats")
            b.emit("vector", lambda e: e.memset(stt[:], 0.0), inc=pv)
            pv_z = pv.n
            apply_done = []
            store_done = []
            for i in range(NT):
                xs = xsl[i % 3]
                lo = lnsl[i % 2]
                st = stt[:, (i % 2) * 16:(i % 2) * 16 + 16]
                w = [(ps_, apply_done[i - 3])] if i >= 3 else [(pv, pv_z)]
                b.emit("sync", lambda e, xs=xs, i=i: e.dma_start(
                    out=xs[:], in_=x_in[i * 128:(i + 1) * 128, :]),
                    waits=w, inc=sq, inc_k=16)
                ld = sq.n
                ow = [(sg, store_done[i - 2])] if i >= 2 else []
                apply_done.append(ln_tile(xs[:], lo[:], st, [(sq, ld)], ow))
                b.emit("gpsimd", lambda e, lo=lo, i=i: e.dma_start(
                    out=ln1[i * 128:(i + 1) * 128, :], in_=lo[:]),
                    waits=[(ps_, apply_done[-1])], inc=sg, inc_k=16)
                store_done.append(sg.n)
            l1_done_sg = sg.n
            arena.reset(m0)

        dq_done_sg = sg.n
        if "dq" in phases:
            m0 = arena.mark()
            scg = arena.alloc([128, 88 * 16], F32, "scg")
            scd = arena.alloc([128, 16 * IKB], F32, "scd")
            m1 = arena.mark()
            gus_sb = arena.alloc([16, OGU], F16, "gus_sb")
            gus_f = arena.alloc([16, OGU], F32, "gus_f")
            dns_sb = arena.alloc([IKB, D], F16, "dns_sb")
            dns_f = arena.alloc([IKB, D], F32, "dns_f")
            dq_entry = phase_barrier()
            b.emit("sync", lambda e: e.dma_start(out=gus_sb[:], in_=gus[:]),
                   waits=dq_entry, inc=sq, inc_k=16)
            b.emit("sync", lambda e: e.dma_start(out=dns_sb[:], in_=dns[:]),
                   inc=sq, inc_k=16)
            ldsc = sq.n
            b.emit("vector", lambda e: e.tensor_copy(gus_f[:], gus_sb[:]),
                   waits=[(sq, ldsc), (pv, pv_const)])
            b.emit("vector", lambda e: e.tensor_copy(dns_f[:], dns_sb[:]), inc=pv)
            scup = pv.n
            sc_evac = []
            for t in range(min(88 + 16, DQ_N_SC)):
                if t < 88:
                    lhs = gus_f[:, t * 128:(t + 1) * 128]
                    kdim = 16
                    dst = scg[:, t * 16:(t + 1) * 16]
                else:
                    td = t - 88
                    lhs = dns_f[:, td * 128:(td + 1) * 128]
                    kdim = IKB
                    dst = scd[:, td * IKB:(td + 1) * IKB]
                sl_ = (t % 2) * 512
                w = [(pv, scup)] if t == 0 else []
                if t >= 2:
                    w.append((pv, sc_evac[t - 2]))
                b.emit("tensor", lambda e, lhs=lhs, kdim=kdim, sl_=sl_: e.matmul(
                    psF[:, sl_:sl_ + kdim], lhs, ident_f[0:kdim, 0:kdim],
                    start=True, stop=True), waits=w, inc=pt)
                b.emit("vector", lambda e, dst=dst, kdim=kdim, sl_=sl_:
                       e.tensor_scalar(
                    out=dst, in0=psF[:, sl_:sl_ + kdim], scalar1=0.5,
                    scalar2=None, op0=OP.mult), waits=[(pt, pt.n)], inc=pv)
                sc_evac.append(pv.n)
            scT_done = pv.n
            dbg_scg_holder[0] = scg
            dbg_scg_holder[1] = scd
            dq_scT_pv = scT_done

            def dequant(n_tiles, jmax, width, packed_d, sc_img, tag, stage_of,
                        first_pv, first_sg, first_ld_waits=()):
                ngr = width // 128
                pk = [arena.alloc([128, jmax], I32, f"pk{tag}{i}") for i in range(2)]
                u_img = arena.alloc([128, width], I32, f"u{tag}")
                s_img = arena.alloc([128, width], I32, f"s{tag}")
                ubf = arena.alloc([128, width], BF16, f"ub{tag}")
                r1 = arena.alloc([128, width], BF16, f"r1{tag}")
                r6 = arena.alloc([128, width], BF16, f"r6{tag}")
                t1 = arena.alloc([128, width], BF16, f"t1{tag}")
                wt = arena.alloc([128, width], BF16, f"wt{tag}")
                vdone = []
                evac_hist = []
                evac_grp = {}
                for t in range(n_tiles):
                    p = pk[t % 2]
                    w = [(pv, vdone[t - 2])] if t >= 2 else list(first_ld_waits)
                    b.emit("sync", lambda e, p=p, t=t: e.dma_start(
                        out=p[:], in_=packed_d[t * 128:(t + 1) * 128, :]),
                        waits=w, inc=sq, inc_k=16)
                    ld = sq.n
                    fw = [(sq, ld)]
                    if t == 0:
                        fw.append((pv, first_pv))
                        fw.append((sg, first_sg))
                    for j in range(8):
                        b.emit("vector", lambda e, p=p, j=j: e.tensor_scalar(
                            out=u_img[:, j::8], in0=p[:], scalar1=4 * j, scalar2=7,
                            op0=OP.logical_shift_right, op1=OP.bitwise_and),
                            waits=fw if j == 0 else [])
                        b.emit("vector", lambda e, p=p, j=j: e.tensor_scalar(
                            out=s_img[:, j::8], in0=p[:], scalar1=4 * j + 3,
                            scalar2=1,
                            op0=OP.logical_shift_right, op1=OP.bitwise_and))
                    b.emit("vector", lambda e: e.tensor_scalar(
                        out=ubf[:], in0=u_img[:], scalar1=0, scalar2=None,
                        op0=OP.add))
                    b.emit("vector", lambda e: e.tensor_scalar(
                        out=r1[:], in0=u_img[:], scalar1=4, scalar2=0,
                        op0=OP.subtract, op1=OP.max))
                    b.emit("vector", lambda e: e.tensor_scalar(
                        out=r6[:], in0=u_img[:], scalar1=6, scalar2=0,
                        op0=OP.subtract, op1=OP.max))
                    b.emit("vector", lambda e: e.tensor_tensor(
                        out=t1[:], in0=ubf[:], in1=r1[:], op=OP.add))
                    b.emit("vector", lambda e: e.tensor_tensor(
                        out=r6[:], in0=r6[:], in1=r6[:], op=OP.add))
                    b.emit("vector", lambda e: e.tensor_tensor(
                        out=t1[:], in0=t1[:], in1=r6[:], op=OP.add))
                    b.emit("vector", lambda e: e.tensor_scalar(
                        out=ubf[:], in0=s_img[:], scalar1=-2, scalar2=1,
                        op0=OP.mult, op1=OP.add))
                    b.emit("vector", lambda e: e.tensor_tensor(
                        out=t1[:], in0=t1[:], in1=ubf[:], op=OP.mult), inc=pv)
                    vdone.append(pv.n)
                    stg_ap, col_f, store, store_wait_extra, wr_wait = stage_of(t)
                    for g in range(ngr):
                        b.emit("vector", lambda e, t=t, g=g: e.tensor_scalar(
                            out=wt[:, g * 128:(g + 1) * 128],
                            in0=t1[:, g * 128:(g + 1) * 128],
                            scalar1=sc_img[:, t * ngr + g:t * ngr + g + 1],
                            scalar2=None, op0=OP.mult))
                    b.emit("vector", lambda e: e.engine_nop(), inc=pv)
                    wt_done = pv.n
                    if DBG_WT is not None and tag == "g" and t < 16:
                        b.emit("gpsimd", lambda e, t=t: e.dma_start(
                            out=DBG_WT[0][t * 128:(t + 1) * 128, :],
                            in_=wt[:]), waits=[(pv, wt_done)], inc=sg, inc_k=16)
                        b.wait("vector", sg, sg.n)
                    for kb in range(ngr):
                        gi = t * ngr + kb
                        bank = gi % 2
                        wts = [(pv, wt_done)] if kb == 0 else []
                        wts.append((pv, evac_grp.get(gi - 2, 0)))
                        evw = list(wr_wait) if kb == 0 else []
                        b.emit("tensor", lambda e, kb=kb, bank=bank, wt=wt:
                               e.transpose(
                                   psB[:, bank * 1024:bank * 1024 + 128],
                                   wt[:, kb * 128:(kb + 1) * 128], ident_b[:]),
                               waits=wts, inc=pt)
                        b.emit("vector", lambda e, c0=col_f(kb), bank=bank,
                               stg_ap=stg_ap: e.tensor_copy(
                            stg_ap[:, c0:c0 + 128],
                            psB[:, bank * 1024:bank * 1024 + 128]),
                            waits=[(pt, pt.n)] + evw, inc=pv)
                        evac_grp[gi] = pv.n
                        evac_hist.append(pv.n)
                    if store is not None:
                        b.emit("gpsimd", lambda e, store=store, stg_ap=stg_ap:
                               e.dma_start(out=store, in_=stg_ap[:]),
                               waits=[(pv, pv.n)] + store_wait_extra,
                               inc=sg, inc_k=16)

            gu_store_sg = {}

            def gu_stage(t):
                c = t // 4
                tloc = t % 4
                stg = stgg[c % 2]
                store = wguT[c, :, :] if tloc == 3 else None
                wr_wait = []
                if tloc == 0 and c >= 2:
                    wr_wait = [(sg, gu_store_sg[c - 2])]
                if store is not None:
                    gu_store_sg[c] = sg.n + 16
                return (stg, (lambda kb, tloc=tloc: tloc * 2048 + kb * 128),
                        store, [], wr_wait)

            if not DQ_SCALES_ONLY:
              arena.reset(m1)
              stgg = [arena.alloc([128, 8192], BF16, f"stgg{i}") for i in range(2)]
              dequant(DQ_NG, D // 8, D, gup, scg, "g", gu_stage, scT_done, sg.n,
                      first_ld_waits=[(pv, scT_done)])
              gate_dq_pv = pv.n
              gate_dq_sg = sg.n
              arena.reset(m1)
              stgd = [arena.alloc([128, IKB * 256], BF16, f"stgd{i}") for i in range(2)]
              dn_store_sg = {}

              def dn_stage(ht):
                  hcc = ht // 2
                  stg = stgd[hcc % 2]
                  store = wdnT[hcc, :, :] if ht % 2 == 1 else None
                  wr_wait = []
                  if ht % 2 == 0 and hcc >= 2:
                      wr_wait = [(sg, dn_store_sg[hcc - 2])]
                  if store is not None:
                      dn_store_sg[hcc] = sg.n + 16
                  return (stg, (lambda kb, ht=ht: kb * 256 + (ht % 2) * 128),
                          store, [], wr_wait)

              dequant(DQ_ND, FI // 8, FI, dnp, scd, "d", dn_stage,
                      gate_dq_pv, gate_dq_sg,
                      first_ld_waits=[(pv, gate_dq_pv), (pt, pt.n)])
            dq_done_sg = sg.n
            arena.reset(m0)

        def ffn_pass(lnsrc, xres, xout, fuse_ln_out, ready_sg, entry):
            entry = entry + phase_barrier()
            m0 = arena.mark()
            lnT = [arena.alloc([128, NKB * NG], BF16, f"f_lnT{i}") for i in range(2)]
            wg = [arena.alloc([128, D], BF16, f"f_wg{i}") for i in range(2)]
            wu = [arena.alloc([128, D], BF16, f"f_wu{i}") for i in range(2)]
            hT = arena.alloc([128, IKB * NG], BF16, "f_hT")
            dnb = [arena.alloc([128, IKB * 256], BF16, f"f_dn{i}") for i in range(2)]
            ysl = arena.alloc([128, 4 * D], F32, "f_y")
            xsl = [arena.alloc([128, D], F32, f"f_x{i}") for i in range(2)]
            sfx = [arena.alloc([128, NG], F32, f"f_sfx{i}") for i in range(2)]
            sqi = arena.alloc([128, 64], F32, "f_sq")
            stt = arena.alloc([128, 32], F32, "f_stats")
            b.emit("vector", lambda e: e.memset(sqi[:], 0.0))
            b.emit("vector", lambda e: e.memset(stt[:], 0.0), inc=pv)
            stats_z = pv.n
            sq_free = [stats_z, stats_z]
            lnout = [arena.alloc([128, D], BF16, f"f_lno{i}") for i in range(2)]

            lnT_free = [0, 0]
            silu_done = {}
            tt_done = {}
            gu_mm_done = {}
            x_free = [0, 0]
            x_ld = [0, 0]
            lno_free = [0, 0]
            y_free_pv = 0
            y_free_ps = 0
            y_free_sg = 0
            hT_free = 0
            dn_free = [0, 0]
            dn_evac = {}
            dn_sq = {}
            gu_ld = {}
            dn_ld = {}
            lnT_ld = [0, 0]

            for g in range(FFN_NGRP):
                sl = g % 2
                w0 = [(pt, lnT_free[sl])]
                if g == 0:
                    w0 = w0 + [(sg, ready_sg)] + entry
                for kb in range(NKB):
                    b.emit("scalar", lambda e, g=g, kb=kb, sl=sl:
                           e.dma_start_transpose(
                               out=lnT[sl][:, kb * NG:(kb + 1) * NG],
                               in_=lnsrc[g * NG:(g + 1) * NG,
                                         kb * 128:(kb + 1) * 128]),
                           waits=w0 if kb == 0 else [], inc=sc, inc_k=16)
                lnT_ld[sl] = sc.n
                for nt in range(2):
                    slx = nt % 2
                    xw = [(pv, x_free[slx])]
                    if g == 0 and nt == 0:
                        xw = xw + [(sg, ready_sg)] + entry
                    b.emit("sync", lambda e, g=g, nt=nt, slx=slx: e.dma_start(
                        out=xsl[slx][:],
                        in_=xres[g * NG + nt * 128:g * NG + (nt + 1) * 128, :]),
                        waits=xw, inc=sq, inc_k=16)
                    x_ld[slx] = sq.n
                for ikb in (range(IKB) if "p1" in FFN_PARTS else ()):
                    pr, tloc = divmod(ikb, 4)
                    for which, (arr, blk) in enumerate(
                            ((wg, pr), (wu, pr + NPAIR))):
                        w = []
                        if g == 0 and ikb < 2:
                            w.append((sg, ready_sg))
                        prev = g * IKB + ikb - 2
                        if prev >= 0:
                            w.append((pt, gu_mm_done.get(prev, 0)))
                        b.emit("sync", lambda e, blk=blk, tloc=tloc, arr=arr,
                               slw=ikb % 2: e.dma_start(
                            out=arr[slw][:],
                            in_=wguT[blk, :, tloc * 2048:(tloc + 1) * 2048]),
                            waits=w if which == 0 else [], inc=sq, inc_k=16)
                    gu_ld[g * IKB + ikb] = sq.n
                    par = ikb % 2
                    slw = ikb % 2
                    gld = gu_ld[g * IKB + ikb]
                    gps = psF[:, par * 512:(par + 1) * 512]
                    ups = psF[:, 1024 + par * 512:1024 + (par + 1) * 512]
                    gi = g * IKB + ikb
                    for kb in range(NKB):
                        w = []
                        if kb == 0:
                            w = [(sq, gld), (sc, lnT_ld[sl]),
                                 (ps_, silu_done.get(gi - 2, 0))]
                        b.emit("tensor", lambda e, kb=kb, gps=gps, slw=slw, sl=sl:
                               e.matmul(gps,
                                        wg[slw][:, kb * 128:(kb + 1) * 128],
                                        lnT[sl][:, kb * NG:(kb + 1) * NG],
                                        start=(kb == 0), stop=(kb == NKB - 1)),
                               waits=w)
                    for kb in range(NKB):
                        w = []
                        if kb == 0:
                            w = [(pv, tt_done.get(gi - 2, 0))]
                        b.emit("tensor", lambda e, kb=kb, ups=ups, slw=slw, sl=sl:
                               e.matmul(ups,
                                        wu[slw][:, kb * 128:(kb + 1) * 128],
                                        lnT[sl][:, kb * NG:(kb + 1) * NG],
                                        start=(kb == 0), stop=(kb == NKB - 1)),
                               waits=w, inc=pt if kb == NKB - 1 else None)
                    gu_mm_done[gi] = pt.n
                    if ikb == IKB - 1:
                        lnT_free[sl] = pt.n
                    b.emit("scalar", lambda e, gps=gps, par=par: e.activation(
                        sfx[par][:], gps, AF.Silu),
                        waits=[(pt, gu_mm_done[gi]), (pv, tt_done.get(gi - 2, 0))],
                        inc=ps_)
                    silu_done[gi] = ps_.n
                    ttw = [(ps_, silu_done[gi])]
                    if ikb == 0:
                        ttw.append((pt, hT_free))
                    b.emit("vector", lambda e, ups=ups, par=par, ikb=ikb:
                           e.tensor_tensor(
                               out=hT[:, ikb * NG:(ikb + 1) * NG],
                               in0=sfx[par][:], in1=ups, op=OP.mult),
                           waits=ttw, inc=pv)
                    tt_done[gi] = pv.n
                hT_done = pv.n
                if g == 0 and xout is x1 and DBG_FFN:
                    b.emit("gpsimd", lambda e: e.dma_start(out=dbg_hT[:], in_=hT[:]),
                           waits=[(pv, hT_done)], inc=sg, inc_k=16)
                    b.emit("gpsimd", lambda e: e.dma_start(out=dbg_lnT[:], in_=lnT[0][:]),
                           waits=[(sc, lnT_ld[0])], inc=sg, inc_k=16)

                for hcc in (range(8) if "p2" in FFN_PARTS else ()):
                    sl2 = hcc % 2
                    wdl = []
                    if g == 0 and hcc < 2:
                        wdl.append((sg, ready_sg))
                    if g * 8 + hcc >= 2:
                        wdl.append((pt, dn_free[sl2]))
                    b.emit("sync", lambda e, hcc=hcc, sl2=sl2: e.dma_start(
                        out=dnb[sl2][:], in_=wdnT[hcc, :, :]), waits=wdl,
                        inc=sq, inc_k=16)
                    dn_ld[g * 8 + hcc] = sq.n
                    dld = dn_ld[g * 8 + hcc]
                    if FFN_P2_MODE == "loads":
                        continue
                    for nt in range(4):
                        it = hcc * 4 + nt
                        par2 = it % 2
                        dps = psF[:, 2048 + par2 * 512:2048 + par2 * 512 + 256]
                        for kb in range(IKB):
                            w = []
                            if kb == 0:
                                w = [(sq, dld), (pv, hT_done),
                                     (pv, dn_evac.get((g, it - 2), 0)),
                                     (ps_, dn_sq.get((g, it - 2), 0))]
                                if it == 0:
                                    w.append((pv, y_free_pv))
                                    w.append((ps_, y_free_ps))
                                    w.append((sg, y_free_sg))
                            b.emit("tensor", lambda e, kb=kb, nt=nt, dps=dps,
                                   sl2=sl2: e.matmul(
                                dps,
                                hT[:, kb * NG + nt * 128:kb * NG + (nt + 1) * 128],
                                dnb[sl2][:, kb * 256:(kb + 1) * 256],
                                start=(kb == 0), stop=(kb == IKB - 1)),
                                waits=w, inc=pt if kb == IKB - 1 else None)
                        mmd = pt.n
                        if nt == 3:
                            dn_free[sl2] = mmd
                        if FFN_P2_MODE == "mm":
                            continue
                        if FFN_P2_MODE != "sq":
                            b.emit("vector", lambda e, nt=nt, hcc=hcc, dps=dps:
                                   e.tensor_copy(
                                       ysl[:, nt * D + hcc * 256:
                                           nt * D + (hcc + 1) * 256], dps),
                                   waits=[(pt, mmd)], inc=pv)
                            dn_evac[(g, it)] = pv.n
                        if FFN_P2_MODE == "evac":
                            continue
                        b.emit("scalar", lambda e, nt=nt, hcc=hcc,
                               col=(g % 2) * 32 + nt * 8 + hcc:
                               e.activation(junkS[:, 0:256],
                                            ysl[:, nt * D + hcc * 256:
                                                nt * D + (hcc + 1) * 256],
                                            AF.Square,
                                            accum_out=sqi[:, col:col + 1]),
                               waits=[(pv, dn_evac[(g, it)]),
                                      (pv, sq_free[g % 2])], inc=ps_)
                        dn_sq[(g, it)] = ps_.n
                sq_all = ps_.n
                if g == 0 and xout is x1:
                    b.emit("gpsimd", lambda e: e.dma_start(out=dbg_y[:], in_=ysl[:]),
                           waits=[(pv, pv.n), (ps_, sq_all)], inc=sg, inc_k=16)

                for nt in (range(4) if "tails" in FFN_PARTS else ()):
                    slx = nt % 2
                    if nt >= 2:
                        b.emit("sync", lambda e, g=g, nt=nt, slx=slx: e.dma_start(
                            out=xsl[slx][:],
                            in_=xres[g * NG + nt * 128:
                                     g * NG + (nt + 1) * 128, :]),
                            waits=[(pv, x_free[slx])], inc=sq, inc_k=16)
                        x_ld[slx] = sq.n
                    so = (g % 2) * 16
                    sqc = sqi[:, (g % 2) * 32 + nt * 8:(g % 2) * 32 + nt * 8 + 8]
                    b.emit("vector", lambda e, sqc=sqc, so=so: e.reduce_sum(
                        out=stt[:, so + 0:so + 1], in_=sqc, axis=AX),
                        waits=[(ps_, sq_all)], inc=pv)
                    b.emit("vector", lambda e, so=so: e.tensor_scalar(
                        out=stt[:, so + 1:so + 2], in0=stt[:, so + 0:so + 1],
                        scalar1=1.0 / D, scalar2=RMS_EPS, op0=OP.mult,
                        op1=OP.add), waits=[(pv, pv.n)], inc=pv)
                    b.emit("scalar", lambda e, so=so: e.activation(
                        stt[:, so + 2:so + 3], stt[:, so + 1:so + 2], AF.Ln),
                        waits=[(pv, pv.n)])
                    b.emit("scalar", lambda e, so=so: e.activation(
                        stt[:, so + 3:so + 4], stt[:, so + 2:so + 3], AF.Exp,
                        scale=-0.5, bias=ln05[:]), inc=ps_)
                    b.emit("vector", lambda e, nt=nt, so=so, slx=slx:
                           e.scalar_tensor_tensor(
                               out=ysl[:, nt * D:(nt + 1) * D],
                               in0=ysl[:, nt * D:(nt + 1) * D],
                               scalar=stt[:, so + 3:so + 4], in1=xsl[slx][:],
                               op0=OP.mult, op1=OP.add),
                           waits=[(ps_, ps_.n), (sq, x_ld[slx])], inc=pv)
                    xo_done = pv.n
                    x_free[slx] = xo_done
                    b.emit("gpsimd", lambda e, g=g, nt=nt: e.dma_start(
                        out=xout[g * NG + nt * 128:g * NG + (nt + 1) * 128, :],
                        in_=ysl[:, nt * D:(nt + 1) * D]),
                        waits=[(pv, xo_done)], inc=sg, inc_k=16)
                    xo_sg = sg.n
                    if fuse_ln_out is not None:
                        slo = nt % 2
                        st2 = stt[:, so + 4:so + 16]
                        ow = [(sg, lno_free[slo])]
                        ap_ps = ln_tile(ysl[:, nt * D:(nt + 1) * D],
                                        lnout[slo][:], st2,
                                        [(pv, xo_done)], ow)
                        b.emit("gpsimd", lambda e, g=g, nt=nt, slo=slo:
                               e.dma_start(
                            out=fuse_ln_out[g * NG + nt * 128:
                                            g * NG + (nt + 1) * 128, :],
                            in_=lnout[slo][:]),
                            waits=[(ps_, ap_ps)], inc=sg, inc_k=16)
                        lno_free[slo] = sg.n
                if g == 0 and xout is x1 and DBG_FFN:
                    b.emit("gpsimd", lambda e: e.dma_start(
                        out=dbg_sq[0:128, 0:64], in_=sqi[:]),
                        waits=[(pv, pv.n), (ps_, ps_.n)], inc=sg, inc_k=16)
                    b.emit("gpsimd", lambda e: e.dma_start(
                        out=dbg_sq[0:128, 64:96], in_=stt[:]), inc=sg, inc_k=16)
                sq_free[g % 2] = pv.n
                y_free_pv = pv.n
                y_free_ps = ps_.n
                y_free_sg = sg.n
                hT_free = pt.n
            done_sg = sg.n
            arena.reset(m0)
            return done_sg

        ffn1_done = l1_done_sg
        if "ffn1" in phases:
            ffn1_done = ffn_pass(ln1, x_in, x1, ln2,
                                 max(l1_done_sg, dq_done_sg), entry_waits())

        a1_done = ffn1_done
        if "a1" in phases:
            m0 = arena.mark()
            wimg = arena.alloc([128, NKB * D], BF16, "a1_w")
            lnTa = [arena.alloc([128, NKB * NGA], BF16, f"a1_lnT{i}")
                    for i in range(2)]
            stg = [arena.alloc([128, 512], BF16, f"a1_stg{i}") for i in range(4)]
            bv_rep = arena.alloc([128, D], F32, "a1_bvrep")
            bvrow = arena.alloc([1, D], F32, "a1_bvrow")
            a1_entry = phase_barrier()
            b.emit("sync", lambda e: e.dma_start(out=bvrow[:], in_=bv[:]),
                   waits=a1_entry, inc=sq, inc_k=16)
            ldbv = sq.n
            for c in range(4):
                bnk = (c % 2) * 512
                b.emit("tensor", lambda e, c=c, bnk=bnk: e.matmul(
                    psF[:, bnk:bnk + 512], ones1[:, 0:128],
                    bvrow[0:1, c * 512:(c + 1) * 512], start=True, stop=True),
                    waits=[(sq, ldbv), (pv, pv_const)] if c == 0 else
                    [(pv, pv.n - 1)], inc=pt)
                b.emit("vector", lambda e, c=c, bnk=bnk: e.tensor_copy(
                    bv_rep[:, c * 512:(c + 1) * 512], psF[:, bnk:bnk + 512]),
                    waits=[(pt, pt.n)], inc=pv)
            bvrep_done = pv.n
            lnT_ld2 = [0, 0]
            lnT_free2 = [0, 0]
            stg_free = [0, 0, 0, 0]
            stg_sg = [0, 0, 0, 0]
            wimg_free = 0
            for wi, (wsrc, dst_d, bias_sb) in enumerate(
                    (("q", qt_d, bq_sb), ("k", kt_d, bk_sb), ("v", v_d, None))):
                w = [(sg, sg_wcast), (pt, wimg_free)]
                if wi == 0:
                    w = w + a1_entry
                for kb in range(NKB):
                    b.emit("scalar", lambda e, kb=kb, wsrc=wsrc:
                           e.dma_start_transpose(
                               out=wimg[:, kb * D:(kb + 1) * D],
                               in_=wbf[wsrc][:, kb * 128:(kb + 1) * 128]),
                           waits=w if kb == 0 else [], inc=sc, inc_k=16)
                wld = sc.n
                for g in range(4):
                    sl = g % 2
                    w2 = [(pt, lnT_free2[sl])]
                    if wi == 0 and g == 0:
                        w2.append((sg, a1_done))
                    for kb in range(NKB):
                        b.emit("scalar", lambda e, g=g, kb=kb, sl=sl:
                               e.dma_start_transpose(
                                   out=lnTa[sl][:, kb * NGA:(kb + 1) * NGA],
                                   in_=ln2[g * NGA:(g + 1) * NGA,
                                           kb * 128:(kb + 1) * 128]),
                               waits=w2 if kb == 0 else [], inc=sc, inc_k=16)
                    lnT_ld2[sl] = sc.n
                    if wsrc != "v":
                        for dt_i in range(16):
                            for ncq in range(NGA // 512):
                                it = dt_i * 2 + ncq
                                par = it % 4
                                pp = psF[:, par * 512:(par + 1) * 512]
                                for kb in range(NKB):
                                    w3 = []
                                    if kb == 0:
                                        w3 = [(sc, max(wld, lnT_ld2[sl])),
                                              (pv, max(stg_free[par],
                                                       bvrep_done))]
                                    b.emit("tensor", lambda e, kb=kb, pp=pp,
                                           dt_i=dt_i, ncq=ncq, sl=sl: e.matmul(
                                        pp,
                                        wimg[:, kb * D + dt_i * 128:
                                             kb * D + (dt_i + 1) * 128],
                                        lnTa[sl][:, kb * NGA + ncq * 512:
                                                 kb * NGA + (ncq + 1) * 512],
                                        start=(kb == 0), stop=(kb == NKB - 1)),
                                        waits=w3,
                                        inc=pt if kb == NKB - 1 else None)
                                mmd = pt.n
                                b.emit("vector", lambda e, pp=pp, par=par,
                                       dt_i=dt_i, bias_sb=bias_sb: e.tensor_scalar(
                                    out=stg[par][:], in0=pp,
                                    scalar1=bias_sb[:, dt_i:dt_i + 1],
                                    scalar2=None, op0=OP.add),
                                    waits=[(pt, mmd), (sg, stg_sg[par])], inc=pv)
                                evd = pv.n
                                b.emit("gpsimd", lambda e, dst_d=dst_d, dt_i=dt_i,
                                       g=g, ncq=ncq, par=par: e.dma_start(
                                    out=dst_d[dt_i, :,
                                              g * NGA + ncq * 512:
                                              g * NGA + (ncq + 1) * 512],
                                    in_=stg[par][:]),
                                    waits=[(pv, evd)], inc=sg, inc_k=16)
                                stg_sg[par] = sg.n
                                stg_free[par] = evd
                    else:
                        for nt in range(NGA // 128):
                            for dc in range(4):
                                it = nt * 4 + dc
                                par = it % 4
                                pp = psF[:, par * 512:(par + 1) * 512]
                                for kb in range(NKB):
                                    w3 = []
                                    if kb == 0:
                                        w3 = [(sc, max(wld, lnT_ld2[sl])),
                                              (pv, max(stg_free[par],
                                                       bvrep_done))]
                                    b.emit("tensor", lambda e, kb=kb, pp=pp,
                                           nt=nt, dc=dc, sl=sl: e.matmul(
                                        pp,
                                        lnTa[sl][:, kb * NGA + nt * 128:
                                                 kb * NGA + (nt + 1) * 128],
                                        wimg[:, kb * D + dc * 512:
                                             kb * D + (dc + 1) * 512],
                                        start=(kb == 0), stop=(kb == NKB - 1)),
                                        waits=w3,
                                        inc=pt if kb == NKB - 1 else None)
                                mmd = pt.n
                                b.emit("vector", lambda e, pp=pp, par=par, dc=dc:
                                       e.tensor_tensor(
                                    out=stg[par][:], in0=pp,
                                    in1=bv_rep[:, dc * 512:(dc + 1) * 512],
                                    op=OP.add),
                                    waits=[(pt, mmd), (sg, stg_sg[par])], inc=pv)
                                evd = pv.n
                                b.emit("gpsimd", lambda e, g=g, nt=nt, dc=dc,
                                       par=par: e.dma_start(
                                    out=v_d[g * 8 + nt, :,
                                            dc * 512:(dc + 1) * 512],
                                    in_=stg[par][:]),
                                    waits=[(pv, evd)], inc=sg, inc_k=16)
                                stg_sg[par] = sg.n
                                stg_free[par] = evd
                    lnT_free2[sl] = pt.n
                wimg_free = pt.n
            a1_done = sg.n
            arena.reset(m0)

        a2_done = a1_done
        if "a2" in phases:
            m0 = arena.mark()
            vimg = arena.alloc([128, NT * D], BF16, "a2_v")
            expT = arena.alloc([128, NT * 512], BF16, "a2_expT")
            ktb = arena.alloc([128, S], BF16, "a2_kt")
            qtb = arena.alloc([128, S], BF16, "a2_qt")
            sacc = [arena.alloc([128, 512], F32, f"a2_sacc{i}") for i in range(2)]
            invb = [arena.alloc([128, 512], F32, f"a2_inv{i}") for i in range(2)]
            mstg = [arena.alloc([128, 512], BF16, f"a2_mst{i}") for i in range(2)]
            a2_entry = phase_barrier()
            for kt in range(NT):
                b.emit("sync", lambda e, kt=kt: e.dma_start(
                    out=vimg[:, kt * D:(kt + 1) * D], in_=v_d[kt, :, :]),
                    waits=([(sg, a1_done)] + a2_entry) if kt == 0 else [],
                    inc=sq, inc_k=16)
            vld = sq.n
            kq_ld = 0
            kq_free = 0
            expT_free = 0
            sacc_free = [0, 0]
            spp_free = [0, 0]
            inv_rd = [0, 0]
            aps_free = [0, 0]
            mst_sg = [0, 0]
            exp_hist = {}
            for h in range(A2_HEADS):
                w = [(sg, a1_done), (pt, kq_free)]
                b.emit("sync", lambda e, h=h: e.dma_start(
                    out=ktb[:], in_=kt_d[h, :, :]), waits=w, inc=sq, inc_k=16)
                b.emit("sync", lambda e, h=h: e.dma_start(
                    out=qtb[:], in_=qt_d[h, :, :]), inc=sq, inc_k=16)
                kq_ld = sq.n
                for qc in range(A2_QCS):
                    pi = qc % 2
                    for kt in range(NT):
                        spar = kt % 2
                        sps = psF[:, spar * 512:(spar + 1) * 512]
                        w3 = []
                        if kt == 0:
                            w3.append((sq, kq_ld))
                        if kt >= 2:
                            w3.append((ps_, exp_hist.get(kt - 2, 0)))
                        else:
                            w3.append((ps_, exp_hist.get(kt - 2 + NT, 0)))
                        b.emit("tensor", lambda e, kt=kt, sps=sps, qc=qc:
                               e.matmul(sps, ktb[:, kt * 128:(kt + 1) * 128],
                                        qtb[:, qc * 512:(qc + 1) * 512],
                                        start=True, stop=True),
                               waits=w3, inc=pt)
                        mmd = pt.n
                        ew = [(pt, mmd)]
                        if kt == 0:
                            ew.append((pt, expT_free))
                        b.emit("scalar", lambda e, kt=kt, sps=sps: e.activation(
                            expT[:, kt * 512:(kt + 1) * 512], sps, AF.Exp,
                            scale=isc[:]), waits=ew, inc=ps_)
                        exp_hist[kt] = ps_.n
                        if kt == 0:
                            b.emit("vector", lambda e, pi=pi: e.tensor_copy(
                                sacc[pi][:], expT[:, 0:512]),
                                waits=[(ps_, ps_.n), (pt, sacc_free[pi])])
                        else:
                            b.emit("vector", lambda e, kt=kt, pi=pi:
                                   e.tensor_tensor(
                                out=sacc[pi][:], in0=sacc[pi][:],
                                in1=expT[:, kt * 512:(kt + 1) * 512], op=OP.add),
                                waits=[(ps_, ps_.n)],
                                inc=pv if kt == NT - 1 else None)
                    sacc_done = pv.n
                    exp_all = ps_.n
                    spp = psF[:, 2048 + pi * 512:2048 + (pi + 1) * 512]
                    b.emit("tensor", lambda e, spp=spp, pi=pi: e.matmul(
                        spp, onesf[:], sacc[pi][:], start=True, stop=True),
                        waits=[(pv, sacc_done), (ps_, spp_free[pi])], inc=pt)
                    sacc_free[pi] = pt.n
                    b.emit("scalar", lambda e, spp=spp, pi=pi: e.activation(
                        invb[pi][:], spp, AF.Ln),
                        waits=[(pt, pt.n), (pv, inv_rd[pi])], inc=ps_)
                    spp_free[pi] = ps_.n
                    b.emit("scalar", lambda e, pi=pi: e.activation(
                        invb[pi][:], invb[pi][:], AF.Exp, scale=-1.0), inc=ps_)
                    inv_done = ps_.n
                    aps = psF[:, 1024 + pi * 512:1024 + (pi + 1) * 512]
                    for kt in range(NT):
                        w3 = []
                        if kt == 0:
                            w3 = [(ps_, exp_all), (sq, vld),
                                  (pv, aps_free[pi])]
                        b.emit("tensor", lambda e, kt=kt, aps=aps, h=h: e.matmul(
                            aps, vimg[:, kt * D + h * 128:kt * D + (h + 1) * 128],
                            expT[:, kt * 512:(kt + 1) * 512],
                            start=(kt == 0), stop=(kt == NT - 1)),
                            waits=w3, inc=pt if kt == NT - 1 else None)
                    av_all = pt.n
                    expT_free = av_all
                    if qc == A2_QCS - 1:
                        kq_free = av_all
                    b.emit("vector", lambda e, aps=aps, pi=pi: e.tensor_tensor(
                        out=mstg[pi][:], in0=aps, in1=invb[pi][:], op=OP.mult),
                        waits=[(pt, av_all), (ps_, inv_done),
                               (sg, mst_sg[pi])], inc=pv)
                    nrm_done = pv.n
                    inv_rd[pi] = nrm_done
                    aps_free[pi] = nrm_done
                    b.emit("gpsimd", lambda e, h=h, qc=qc, pi=pi: e.dma_start(
                        out=mgd[h, :, qc * 512:(qc + 1) * 512], in_=mstg[pi][:]),
                        waits=[(pv, nrm_done)], inc=sg, inc_k=16)
                    mst_sg[pi] = sg.n
            a2_done = sg.n
            arena.reset(m0)

        a3_done = a2_done
        if "a3" in phases:
            m0 = arena.mark()
            wimg = arena.alloc([128, NKB * D], BF16, "a3_w")
            mimg = [arena.alloc([128, NKB * NGA], BF16, f"a3_m{i}")
                    for i in range(2)]
            bo_rep = arena.alloc([128, D], F32, "a3_borep")
            borow = arena.alloc([1, D], F32, "a3_borow")
            x1sl = [arena.alloc([128, D], F32, f"a3_x{i}") for i in range(2)]
            xosl = [arena.alloc([128, D], F32, f"a3_xo{i}") for i in range(2)]
            lnosl = [arena.alloc([128, D], BF16, f"a3_lno{i}") for i in range(2)]
            stt3 = arena.alloc([128, 32], F32, "a3_stats")
            b.emit("vector", lambda e: e.memset(stt3[:], 0.0), inc=pv)
            stats3_z = pv.n
            a3_entry = phase_barrier()
            b.emit("sync", lambda e: e.dma_start(out=borow[:], in_=bo[:]),
                   waits=a3_entry, inc=sq, inc_k=16)
            ldbo = sq.n
            for c in range(4):
                bnk = (c % 2) * 512
                b.emit("tensor", lambda e, c=c, bnk=bnk: e.matmul(
                    psF[:, bnk:bnk + 512], ones1[:, 0:128],
                    borow[0:1, c * 512:(c + 1) * 512], start=True, stop=True),
                    waits=[(sq, ldbo), (pv, pv_const)] if c == 0 else
                    [(pv, pv.n - 1)], inc=pt)
                b.emit("vector", lambda e, c=c, bnk=bnk: e.tensor_copy(
                    bo_rep[:, c * 512:(c + 1) * 512], psF[:, bnk:bnk + 512]),
                    waits=[(pt, pt.n)], inc=pv)
            borep_done = pv.n
            for kb in range(NKB):
                b.emit("scalar", lambda e, kb=kb: e.dma_start_transpose(
                    out=wimg[:, kb * D:(kb + 1) * D],
                    in_=wbf["o"][:, kb * 128:(kb + 1) * 128]),
                    waits=([(sg, sg_wcast)] + a3_entry) if kb == 0 else [],
                    inc=sc, inc_k=16)
            wld = sc.n
            m_ld = [0, 0]
            m_free = [0, 0]
            x1_ld = [0, 0]
            x1_free = [0, 0]
            xo_free = [0, 0]
            xo_ps = [0, 0]
            xo_sg = [0, 0]
            lno_free3 = [0, 0]
            evac_pv = []
            itg = 0
            for g in range(4):
                sl = g % 2
                w = [(pt, m_free[sl])]
                if g == 0:
                    w.append((sg, a2_done))
                for kb in range(NKB):
                    b.emit("sync", lambda e, g=g, kb=kb, sl=sl: e.dma_start(
                        out=mimg[sl][:, kb * NGA:(kb + 1) * NGA],
                        in_=mgd[kb, :, g * NGA:(g + 1) * NGA]),
                        waits=w if kb == 0 else [], inc=sq, inc_k=16)
                m_ld[sl] = sq.n
                for nt in range(NGA // 128):
                    slx = nt % 2
                    b.emit("sync", lambda e, g=g, nt=nt, slx=slx: e.dma_start(
                        out=x1sl[slx][:],
                        in_=x1[g * NGA + nt * 128:g * NGA + (nt + 1) * 128, :]),
                        waits=[(pv, x1_free[slx])], inc=sq, inc_k=16)
                    x1_ld[slx] = sq.n
                    for hh in range(2):
                        par = itg % 3
                        pp = psF[:, par * 1024:(par + 1) * 1024]
                        for kb in range(NKB):
                            for sub in range(2):
                                w3 = []
                                if kb == 0 and sub == 0:
                                    w3 = [(sq, m_ld[sl]), (sc, wld),
                                          (pv, borep_done)]
                                    if itg >= 3:
                                        w3.append((pv, evac_pv[itg - 3]))
                                b.emit("tensor", lambda e, kb=kb, sub=sub, pp=pp,
                                       nt=nt, hh=hh, sl=sl: e.matmul(
                                    pp[:, sub * 512:(sub + 1) * 512],
                                    mimg[sl][:, kb * NGA + nt * 128:
                                             kb * NGA + (nt + 1) * 128],
                                    wimg[:, kb * D + hh * 1024 + sub * 512:
                                         kb * D + hh * 1024 + (sub + 1) * 512],
                                    start=(kb == 0), stop=(kb == NKB - 1)),
                                    waits=w3,
                                    inc=pt if (kb == NKB - 1 and sub == 1) else None)
                        mmd = pt.n
                        if nt == NGA // 128 - 1 and hh == 1:
                            m_free[sl] = mmd
                        ew = [(pt, mmd), (sq, x1_ld[slx])]
                        if hh == 0:
                            ew.append((pv, xo_free[slx]))
                            ew.append((ps_, xo_ps[slx]))
                            ew.append((sg, xo_sg[slx]))
                        b.emit("vector", lambda e, pp=pp, hh=hh, slx=slx:
                               e.tensor_tensor(
                            out=xosl[slx][:, hh * 1024:(hh + 1) * 1024],
                            in0=pp, in1=bo_rep[:, hh * 1024:(hh + 1) * 1024],
                            op=OP.add), waits=ew)
                        b.emit("vector", lambda e, hh=hh, slx=slx: e.tensor_tensor(
                            out=xosl[slx][:, hh * 1024:(hh + 1) * 1024],
                            in0=xosl[slx][:, hh * 1024:(hh + 1) * 1024],
                            in1=x1sl[slx][:, hh * 1024:(hh + 1) * 1024],
                            op=OP.add), inc=pv)
                        evac_pv.append(pv.n)
                        itg += 1
                    xo_done = pv.n
                    x1_free[slx] = xo_done
                    b.emit("gpsimd", lambda e, g=g, nt=nt, slx=slx: e.dma_start(
                        out=x2[g * NGA + nt * 128:g * NGA + (nt + 1) * 128, :],
                        in_=xosl[slx][:]), waits=[(pv, xo_done)],
                        inc=sg, inc_k=16)
                    xo_sg[slx] = sg.n
                    slo = nt % 2
                    st3 = stt3[:, slo * 16:slo * 16 + 16]
                    ow = [(sg, lno_free3[slo])]
                    ap_ps = ln_tile(xosl[slx][:], lnosl[slo][:], st3,
                                    [(pv, xo_done)], ow)
                    xo_free[slx] = pv.n
                    xo_ps[slx] = ap_ps
                    b.emit("gpsimd", lambda e, g=g, nt=nt, slo=slo: e.dma_start(
                        out=ln3[g * NGA + nt * 128:g * NGA + (nt + 1) * 128, :],
                        in_=lnosl[slo][:]), waits=[(ps_, ap_ps)],
                        inc=sg, inc_k=16)
                    lno_free3[slo] = sg.n
            a3_done = sg.n
            arena.reset(m0)

        if "ffn2" in phases:
            ffn_pass(ln3, x2, out, None, a3_done, entry_waits())

        if debug_out == "wt":
            debug_out = None

        if debug_out == "ht":
            phase_barrier()
            m0 = arena.mark()
            t1f = arena.alloc([128, 2048], F32, "dbgh1")
            t0h = arena.alloc([128, 2048], BF16, "dbgh0")
            for j in range(11):
                b.emit("gpsimd", lambda e, j=j: e.dma_start(
                    out=t0h[:], in_=dbg_hT[:, j * 2048:(j + 1) * 2048]),
                    waits=[(sg, sg.n)], inc=sg, inc_k=16)
                b.emit("vector", lambda e, j=j: e.tensor_copy(
                    t1f[:], t0h[:]), waits=[(sg, sg.n)], inc=pv)
                b.emit("gpsimd", lambda e, j=j: e.dma_start(
                    out=out[j * 128:(j + 1) * 128, :], in_=t1f[:]),
                    waits=[(pv, pv.n)], inc=sg, inc_k=16)
                b.wait("gpsimd", sg, sg.n)
            debug_out = None

        if debug_out == "sqi":
            phase_barrier()
            m0q = arena.mark()
            tq = arena.alloc([128, 96], F32, "dbgq1")
            b.emit("gpsimd", lambda e: e.dma_start(out=tq[:], in_=dbg_sq[:]),
                   waits=[(sg, sg.n)], inc=sg, inc_k=16)
            b.emit("gpsimd", lambda e: e.dma_start(out=out[0:128, 0:96], in_=tq[:]),
                   waits=[(sg, sg.n)], inc=sg, inc_k=16)
            b.wait("gpsimd", sg, sg.n)
            debug_out = None

        if debug_out == "mgda":
            phase_barrier()
            m0r = arena.mark()
            t0r = arena.alloc([128, 4096], BF16, "dbga0")
            t1r = arena.alloc([128, 2048], F32, "dbga1")
            for hh2 in range(16):
                b.emit("gpsimd", lambda e, hh2=hh2: e.dma_start(
                    out=t0r[:], in_=mgd[hh2, :, :]), waits=[(sg, sg.n)],
                    inc=sg, inc_k=16)
                for j in range(2):
                    b.emit("vector", lambda e, j=j: e.tensor_copy(
                        t1r[:], t0r[:, j * 2048:(j + 1) * 2048]),
                        waits=[(sg, sg.n)], inc=pv)
                    b.emit("gpsimd", lambda e, hh2=hh2, j=j: e.dma_start(
                        out=out[(hh2 * 2 + j) * 128:(hh2 * 2 + j + 1) * 128, :],
                        in_=t1r[:]), waits=[(pv, pv.n)], inc=sg, inc_k=16)
                    b.wait("gpsimd", sg, sg.n)
            debug_out = None

        if debug_out in ("qt0", "kt0", "v0", "mgd0"):
            srcmap = {"qt0": (qt_d, DBG_HEAD), "kt0": (kt_d, DBG_HEAD), "v0": (v_d, None),
                      "mgd0": (mgd, DBG_HEAD)}
            srct, idx = srcmap[debug_out]
            phase_barrier()
            m0r = arena.mark()
            t0r = arena.alloc([128, 4096], BF16, "dbgr0")
            t1r = arena.alloc([128, 2048], F32, "dbgr1")
            if idx is not None:
                b.emit("gpsimd", lambda e, srct=srct, idx=idx: e.dma_start(
                    out=t0r[:], in_=srct[idx, :, :]), waits=[(sg, sg.n)],
                    inc=sg, inc_k=16)
                for j in range(2):
                    b.emit("vector", lambda e, j=j: e.tensor_copy(
                        t1r[:], t0r[:, j * 2048:(j + 1) * 2048]),
                        waits=[(sg, sg.n)], inc=pv)
                    b.emit("gpsimd", lambda e, j=j: e.dma_start(
                        out=out[j * 128:(j + 1) * 128, :], in_=t1r[:]),
                        waits=[(pv, pv.n)], inc=sg, inc_k=16)
                    b.wait("gpsimd", sg, sg.n)
            else:
                for ti in range(16):
                    b.emit("gpsimd", lambda e, ti=ti: e.dma_start(
                        out=t0r[:, 0:2048], in_=v_d[ti, :, :]),
                        waits=[(sg, sg.n)], inc=sg, inc_k=16)
                    b.emit("vector", lambda e: e.tensor_copy(
                        t1r[:], t0r[:, 0:2048]), waits=[(sg, sg.n)], inc=pv)
                    b.emit("gpsimd", lambda e, ti=ti: e.dma_start(
                        out=out[ti * 128:(ti + 1) * 128, :], in_=t1r[:]),
                        waits=[(pv, pv.n)], inc=sg, inc_k=16)
                    b.wait("gpsimd", sg, sg.n)
            debug_out = None

        if debug_out == "y":
            phase_barrier()
            m0y = arena.mark()
            t1fy = arena.alloc([128, 2048], F32, "dbgy1")
            for j in range(4):
                b.emit("gpsimd", lambda e, j=j: e.dma_start(
                    out=t1fy[:], in_=dbg_y[:, j * 2048:(j + 1) * 2048]),
                    waits=[(sg, sg.n)], inc=sg, inc_k=16)
                b.emit("gpsimd", lambda e, j=j: e.dma_start(
                    out=out[j * 128:(j + 1) * 128, :], in_=t1fy[:]),
                    waits=[(sg, sg.n)], inc=sg, inc_k=16)
                b.wait("gpsimd", sg, sg.n)
            debug_out = None

        if debug_out == "scg":
            b.emit("gpsimd", lambda e: e.dma_start(
                out=out[0:128, 0:1408], in_=dbg_scg_holder[0][:]),
                waits=[(pv, dq_scT_pv)], inc=sg, inc_k=16)
            b.emit("gpsimd", lambda e: e.dma_start(
                out=out[128:256, 0:704], in_=dbg_scg_holder[1][:]),
                inc=sg, inc_k=16)
            b.wait("gpsimd", sg, sg.n)
            debug_out = None

        if debug_out == "wgut":
            phase_barrier()
            m0 = arena.mark()
            t0 = arena.alloc([128, 8192], BF16, "dbgw0")
            t1f = arena.alloc([128, 2048], F32, "dbgw1")
            wall = sg.n
            dbg_entry = phase_barrier()
            for c in range(8):
                b.emit("gpsimd", lambda e, c=c: e.dma_start(
                    out=t0[:], in_=wguT[c, :, :]),
                    waits=([(sg, wall)] + dbg_entry) if c == 0 else [(sg, sg.n)],
                    inc=sg, inc_k=16)
                ld = sg.n
                for j in range(4):
                    b.emit("vector", lambda e, j=j: e.tensor_copy(
                        t1f[:], t0[:, j * 2048:(j + 1) * 2048]),
                        waits=[(sg, sg.n)], inc=pv)
                    b.emit("gpsimd", lambda e, c=c, j=j: e.dma_start(
                        out=out[(c * 4 + j) * 128:(c * 4 + j + 1) * 128, :],
                        in_=t1f[:]), waits=[(pv, pv.n)], inc=sg, inc_k=16)
                    b.wait("gpsimd", sg, sg.n)
            debug_out = None

        if debug_out is not None:
            src, sdt = dbg_map[debug_out]
            m0 = arena.mark()
            t0 = arena.alloc([128, D], sdt, "dbg0")
            t1f = arena.alloc([128, D], F32, "dbg1")
            wall = sg.n
            dbg_entry = phase_barrier()
            for i in range(NT):
                b.emit("gpsimd", lambda e, i=i, src=src: e.dma_start(
                    out=t0[:], in_=src[i * 128:(i + 1) * 128, :]),
                    waits=([(sg, wall)] + dbg_entry) if i == 0
                    else [(sg, sg.n)], inc=sg, inc_k=16)
                ld = sg.n
                if sdt == F32:
                    b.emit("gpsimd", lambda e, i=i: e.dma_start(
                        out=out[i * 128:(i + 1) * 128, :], in_=t0[:]),
                        waits=[(sg, ld)], inc=sg, inc_k=16)
                else:
                    b.emit("vector", lambda e: e.tensor_copy(t1f[:], t0[:]),
                           waits=[(sg, ld)], inc=pv)
                    b.emit("gpsimd", lambda e, i=i: e.dma_start(
                        out=out[i * 128:(i + 1) * 128, :], in_=t1f[:]),
                        waits=[(pv, pv.n)], inc=sg, inc_k=16)
                b.wait("gpsimd", sg, sg.n)
            arena.reset(m0)

        b.wait("gpsimd", sg, sg.n)
        b.wait("gpsimd", sq, sq.n)
        b.wait("gpsimd", sc, sc.n)

        block = ctx.enter_context(nc.Block())

        @block.sync
        def _(e):
            for f in b.tape["sync"]:
                f(e)

        @block.gpsimd
        def _(e):
            for f in b.tape["gpsimd"]:
                f(e)

        @block.vector
        def _(e):
            for f in b.tape["vector"]:
                f(e)

        @block.scalar
        def _(e):
            for f in b.tape["scalar"]:
                f(e)

        @block.tensor
        def _(e):
            for f in b.tape["tensor"]:
                f(e)

    return nc


def make_in_maps(inputs):
    x = np.ascontiguousarray(np.asarray(inputs["L0_attn_x"]))
    base = {
        "wq": np.ascontiguousarray(np.asarray(inputs["L0_attn_Wq"])),
        "wk": np.ascontiguousarray(np.asarray(inputs["L0_attn_Wk"])),
        "wv": np.ascontiguousarray(np.asarray(inputs["L0_attn_Wv"])),
        "wo": np.ascontiguousarray(np.asarray(inputs["L0_attn_Wo"])),
        "bq": np.ascontiguousarray(np.asarray(inputs["L0_attn_bq"]).reshape(1, D)),
        "bk": np.ascontiguousarray(np.asarray(inputs["L0_attn_bk"]).reshape(1, D)),
        "bv": np.ascontiguousarray(np.asarray(inputs["L0_attn_bv"]).reshape(1, D)),
        "bo": np.ascontiguousarray(np.asarray(inputs["L0_attn_bo"]).reshape(1, D)),
        "gup": np.ascontiguousarray(np.asarray(inputs["L0_ffn_gate_up_packed"])),
        "gus": np.ascontiguousarray(np.asarray(inputs["L0_ffn_gate_up_scales"])),
        "dnp": np.ascontiguousarray(np.asarray(inputs["L0_ffn_down_packed"])),
        "dns": np.ascontiguousarray(np.asarray(inputs["L0_ffn_down_scales"])),
    }
    return [dict(base, x=np.ascontiguousarray(x[c])) for c in range(8)]


def kernel(**inputs):
    nc = build_nc()
    in_maps = make_in_maps(inputs)
    res = run_bass_kernel_spmd(nc, in_maps, core_ids=list(range(8)))
    return np.stack([r["out"] for r in res.results], axis=0)
